# version 06
# baseline (speedup 1.0000x reference)
import math
import numpy as np
import ml_dtypes
import concourse.bass as bass
import concourse.mybir as mybir
from concourse.bass_utils import run_bass_kernel_spmd

F32 = mybir.dt.float32
BF16 = mybir.dt.bfloat16
AF = mybir.ActivationFunctionType
AX = mybir.AxisListType
ALU = mybir.AluOpType

T = 1024
D = 2048
NCH = 16
NH = 16
HD = 128
NE = 32
FE = 768
FS = 1536
ISQ = 1.0 / math.sqrt(HD)
EPS = 1e-5

ENGS = ("pe", "dve", "act", "pool", "sp")

STOP = None


class Plan:
    def __init__(self):
        self.cmds = {k: [] for k in ENGS}
        from collections import defaultdict
        self.cnt = defaultdict(int)
        self.lastw = {}

    def op(self, eng, fn, inc=None, incn=1):
        t = None
        if inc is not None:
            self.cnt[inc] += incn
            t = self.cnt[inc]
        self.cmds[eng].append(("op", fn, inc, incn))
        return t

    def w(self, eng, sem, tick):
        if tick is None:
            return
        k = (eng, sem)
        if self.lastw.get(k, 0) >= tick:
            return
        self.lastw[k] = tick
        self.cmds[eng].append(("wait", sem, tick))

    def wt(self, eng, tk):
        if tk is not None:
            self.w(eng, tk[0], tk[1])


def psum_alias(nc, name, shape, dtype, addr):
    h = nc.virtual_psum_tensor(name, shape, dtype)
    nc._tensor(h.name, list(shape), dtype, type="PSUM")
    ml = nc.lookup_mloc(h)
    ml.allocated = True
    ml.addr = addr
    return h


def build_nc():
    nc = bass.Bass("TRN2", target_bir_lowering=False)

    xt = nc.dram_tensor("xt", [D, T], F32, kind="ExternalInput")
    wvt = nc.dram_tensor("wvt", [D, D], F32, kind="ExternalInput")
    wot = nc.dram_tensor("wot", [D, D], F32, kind="ExternalInput")
    bvr = nc.dram_tensor("bvr", [128, NCH], F32, kind="ExternalInput")
    bor = nc.dram_tensor("bor", [128, NCH], F32, kind="ExternalInput")
    gw = nc.dram_tensor("gw", [D, NE], F32, kind="ExternalInput")
    wg = nc.dram_tensor("wg", [NE, D, FE], F32, kind="ExternalInput")
    wu = nc.dram_tensor("wu", [NE, D, FE], F32, kind="ExternalInput")
    wd = nc.dram_tensor("wd", [NE, FE, D], F32, kind="ExternalInput")
    sg = nc.dram_tensor("sg", [D, FS], F32, kind="ExternalInput")
    su = nc.dram_tensor("su", [D, FS], F32, kind="ExternalInput")
    sd = nc.dram_tensor("sd", [FS, D], F32, kind="ExternalInput")
    yt = nc.dram_tensor("yt", [D, T], F32, kind="ExternalOutput")

    identf = nc.inline_tensor(np.eye(128, dtype=np.float32), "identf")

    T1 = nc.alloc_sbuf_tensor("T1", [128, NH * T], BF16)
    T2 = nc.alloc_sbuf_tensor("T2", [128, NH * T], BF16)
    T3 = nc.alloc_sbuf_tensor("T3", [128, NH * T], BF16)
    TX = nc.alloc_sbuf_tensor("TX", [128, NCH, T], BF16)
    TC = nc.alloc_sbuf_tensor("TC", [128, NCH, T], BF16)
    RG = nc.alloc_sbuf_tensor("RG", [128, 5, 16, 128], BF16)
    RW = nc.alloc_sbuf_tensor("RW", [128, 16, 512], BF16)

    def at(base, name, shape, dtype, off_bytes):
        a0 = nc.lookup_mloc(base).addr
        return nc.alloc_sbuf_tensor_at(name, shape, dtype, offset=a0 + off_bytes)

    actp = at(T1, "actp", [128, 12, T], BF16, 0)
    silu4 = at(T1, "silu4", [128, 4, 512], F32, 24576)
    sqr2 = at(T1, "sqr2", [128, 4, 512], BF16, 24576)
    sqC = at(T3, "sqC", [128, 4, 512], BF16, 0)
    rows = at(T3, "rows", [1, 3, T], F32, 4096)
    bc = at(T3, "bc", [128, 2, T], F32, 16384)
    call = at(T3, "call", [32, T], F32, 24576)
    crow = at(T3, "crow", [1, T], F32, 28672)
    scrC = at(TX, "scrC", [128, 4, 512], F32, 0)
    acc_lo = at(TX, "acc_lo", [128, 8, T], F32, 0)
    acc_hi = at(TC, "acc_hi", [128, 8, T], F32, 0)

    id_f = nc.alloc_sbuf_tensor("id_f", [128, 128], F32)
    bias_sb = nc.alloc_sbuf_tensor("bias_sb", [128, 2, 16], F32)
    gate_sb = nc.alloc_sbuf_tensor("gate_sb", [128, 16, 32], BF16)
    ones_bf = nc.alloc_sbuf_tensor("ones_bf", [128, 1], BF16)
    ones_f1 = nc.alloc_sbuf_tensor("ones_f1", [1, 128], F32)
    ones_b1 = nc.alloc_sbuf_tensor("ones_b1", [1, 128], BF16)
    VS = nc.alloc_sbuf_tensor("VS", [128, T], BF16)
    bvs = nc.alloc_sbuf_tensor("bvs", [128, 1], F32)
    gsc = nc.alloc_sbuf_tensor("gsc", [128, 2, 32], F32)
    top8 = nc.alloc_sbuf_tensor("top8", [128, 8], F32)
    mskt = nc.alloc_sbuf_tensor("mskt", [128, 2, 32], F32)
    kept = nc.alloc_sbuf_tensor("kept", [128, 2, 32], F32)
    den = nc.alloc_sbuf_tensor("den", [128, 2], F32)
    ctm = nc.alloc_sbuf_tensor("ctm", [128, 2, 32], F32)

    assert nc.sbuf_bytes_remaining >= 0, nc.sbuf_bytes_remaining

    PB = [nc.alloc_psum_tensor(f"PB{k}", [128, 512], F32) for k in range(6)]

    plan = Plan()
    bank_free = [None] * 8

    def acquire(k):
        plan.wt("pe", bank_free[k])

    gu_free = [None] * 5
    gu_i = [0]
    wd_free = [None] * 16
    wd_i = [0]

    def gu_dma(src_ap):
        s = gu_i[0] % 5
        gu_i[0] += 1
        plan.wt("pool", gu_free[s])
        t = plan.op("pool", lambda e, d=RG[:, s, :, :], sp=src_ap: e.dma_start(d, sp),
                    inc=f"gu{s}", incn=16)
        return s, (f"gu{s}", t)

    def wd_dma(src_ap):
        s = wd_i[0] % 16
        wd_i[0] += 1
        plan.wt("pool", wd_free[s])
        t = plan.op("pool", lambda e, d=RW[:, s, :], sp=src_ap: e.dma_start(d, sp),
                    inc=f"wd{s}", incn=16)
        return s, (f"wd{s}", t)

    def dump_and_finish(aps, dep):
        dest3 = yt.ap().rearrange("(c p) t -> p c t", p=128)
        off = 0
        td = None
        for s_ap in aps:
            cols = int(np.prod(s_ap.shape[1:]))
            nch = cols // T
            src3 = bass.AP(s_ap.tensor, s_ap.offset,
                           [s_ap.ap[0], [T, nch], [1, T]])
            psz = s_ap.shape[0]
            plan.wt("pool", dep)
            if nch == 0:
                td = plan.op("pool", lambda e, s=s_ap, d=dest3[0:psz, off, 0:cols]:
                             e.dma_start(d, s, allow_slow_non_contiguous=True),
                             inc="dbg", incn=16)
                off += 1
                continue
            td = plan.op("pool", lambda e, s=src3, d=dest3[0:psz, off:off + nch, :]: e.dma_start(d, s),
                         inc="dbg", incn=16)
            off += nch
        plan.w("sp", "dbg", td)

    tick_misc = None
    for dst, src in ((id_f[:, :], identf[:, :]),
                     (bias_sb[:, 0, :], bvr[:, :]), (bias_sb[:, 1, :], bor[:, :])):
        tick_misc = plan.op("pool", lambda e, d=dst, s=src: e.dma_start(d, s),
                            inc="dmap", incn=16)
    tick_gate = plan.op("pool", lambda e: e.dma_start(
        gate_sb[:, :, :], bass.AP(gw, 0, [[NE, 128], [128 * NE, 16], [1, NE]])),
        inc="dmap", incn=16)
    plan.op("pool", lambda e: e.memset(ones_bf[:, :], 1.0))
    plan.op("pool", lambda e: e.memset(ones_f1[:, :], 1.0))
    plan.op("pool", lambda e: e.memset(ones_b1[:, :], 1.0))
    tick_x0 = None
    for dc in range(NCH):
        tick_x0 = plan.op("pool", lambda e, dc=dc: e.dma_start(
            TX[:, dc, :], xt[dc * 128:(dc + 1) * 128, :]), inc="dmap", incn=16)

    plan.w("dve", "dmap", tick_misc)
    tbv = plan.op("dve", lambda e: e.tensor_reduce(
        bvs[:, 0:1], bias_sb[:, 0, :], AX.X, ALU.add), inc="dve")
    t_vlast = None
    for h in range(NH):
        src = bass.AP(wvt, h * 128, [[D, 128], [128 * D, 16], [1, 128]])
        s, tw = gu_dma(src)
        plan.wt("pe", tw)
        plan.w("pe", "dmap", tick_x0)
        if h == 0:
            acquire(0)
            acquire(1)
        for dc in range(NCH):
            for half in range(2):
                t_vlast = plan.op("pe", lambda e, s=s, dc=dc, half=half, h=h: e.matmul(
                    PB[half][:, :], RG[:, s, dc, :],
                    TX[:, dc, half * 512:(half + 1) * 512],
                    start=(h == 0 and dc == 0), stop=(h == 15 and dc == 15)),
                    inc="pe" if (dc == 15 and half == 1) else None)
        gu_free[s] = ("pe", t_vlast)
    tick_vs = None
    for half in range(2):
        plan.w("dve", "pe", t_vlast)
        plan.w("dve", "dve", tbv)
        tick_vs = plan.op("dve", lambda e, half=half: e.tensor_scalar(
            VS[:, half * 512:(half + 1) * 512], PB[half][:, :], bvs[:, 0:1],
            None, ALU.add), inc="dve")
        bank_free[half] = ("dve", tick_vs)
    if STOP == "vs":
        dump_and_finish([VS[:, :]], ("dve", tick_vs))
        return finish(nc, plan)

    r1_ticks = {}
    for m in range(NCH):
        src = bass.AP(wot, m * 128, [[D, 128], [128 * D, 16], [1, 128]])
        s, tw = gu_dma(src)
        for half in range(2):
            k = (m * 2 + half) % 4
            acquire(k)
            plan.wt("pe", tw)
            plan.w("pe", "dve", tick_vs)
            tmm = None
            for fc in range(NCH):
                tmm = plan.op("pe", lambda e, k=k, s=s, fc=fc, half=half: e.matmul(
                    PB[k][:, :], RG[:, s, fc, :], VS[:, half * 512:(half + 1) * 512],
                    start=(fc == 0), stop=(fc == 15)),
                    inc="pe" if fc == 15 else None)
            if half == 1:
                gu_free[s] = ("pe", tmm)
            r1sl = T1[:, m * T + half * 512: m * T + half * 512 + 512]
            plan.w("dve", "pe", tmm)
            tev = plan.op("dve", lambda e, r1sl=r1sl, k=k, m=m, half=half: e.scalar_tensor_tensor(
                r1sl, PB[k][:, :], bias_sb[:, 1, m:m + 1],
                TX[:, m, half * 512:(half + 1) * 512], ALU.add, ALU.add), inc="dve")
            bank_free[k] = ("dve", tev)
            r1_ticks[(m, half)] = tev

    def ln_rows_and_bcast(ta_last):
        plan.w("dve", "act", ta_last)
        tq = plan.op("dve", lambda e: e.tensor_mul(rows[:, 2, :], rows[:, 0, :], rows[:, 0, :]),
                     inc="dve")
        plan.w("dve", "dve", tq)
        tq = plan.op("dve", lambda e: e.tensor_sub(rows[:, 1, :], rows[:, 1, :], rows[:, 2, :]),
                     inc="dve")
        plan.w("dve", "dve", tq)
        tv = plan.op("dve", lambda e: e.tensor_scalar_add(rows[:, 1, :], rows[:, 1, :], EPS),
                     inc="dve")
        plan.w("act", "dve", tv)
        ts = plan.op("act", lambda e: e.activation(rows[:, 2, :], rows[:, 1, :], AF.Sqrt),
                     inc="act")
        plan.w("dve", "act", ts)
        tq = plan.op("dve", lambda e: e.reciprocal(rows[:, 1, :], rows[:, 2, :]), inc="dve")
        plan.w("dve", "dve", tq)
        trow = plan.op("dve", lambda e: e.tensor_mul(rows[:, 2, :], rows[:, 0, :], rows[:, 1, :]),
                       inc="dve")
        te = None
        for ri, bci in ((1, 0), (2, 1)):
            for half in range(2):
                k = 4 + half
                acquire(k)
                plan.w("pe", "dve", trow)
                tb = plan.op("pe", lambda e, k=k, ri=ri, half=half: e.matmul(
                    PB[k][:, :], ones_f1[:, :], rows[:, ri, half * 512:(half + 1) * 512],
                    start=True, stop=True), inc="pe")
                plan.w("dve", "pe", tb)
                te = plan.op("dve", lambda e, k=k, bci=bci, half=half: e.tensor_copy(
                    bc[:, bci, half * 512:(half + 1) * 512], PB[k][:, :]), inc="dve")
                bank_free[k] = ("dve", te)
        return te

    if STOP == "r1":
        dump_and_finish([T1[:, :]], ("dve", max(r1_ticks.values())))
        return finish(nc, plan)

    sq_free = {}
    ta_last = None
    for half in range(2):
        acquire(4)
        acquire(5)
        tmm2 = None
        for m in range(NCH):
            r1sl = T1[:, m * T + half * 512: m * T + half * 512 + 512]
            tready = r1_ticks[(m, half)]
            plan.w("pe", "dve", tready)
            plan.op("pe", lambda e, r1sl=r1sl, m=m: e.matmul(
                PB[4][0:1, :], ones_bf[:, :], r1sl, start=(m == 0), stop=(m == 15)))
            jq = m % 4
            sqap = sqC[:, jq, :]
            plan.w("act", "dve", tready)
            plan.wt("act", sq_free.get(jq))
            tsq = plan.op("act", lambda e, sqap=sqap, r1sl=r1sl: e.activation(
                sqap, r1sl, AF.Square), inc="act")
            plan.w("pe", "act", tsq)
            tmm2 = plan.op("pe", lambda e, sqap=sqap, m=m: e.matmul(
                PB[5][0:1, :], ones_bf[:, :], sqap, start=(m == 0), stop=(m == 15)),
                inc="pe")
            sq_free[jq] = ("pe", tmm2)
        hs = slice(half * 512, (half + 1) * 512)
        plan.w("act", "pe", tmm2)
        plan.op("act", lambda e, hs=hs: e.activation(
            rows[:, 0, hs], PB[4][0:1, :], AF.Identity, scale=1.0 / D))
        ta_last = plan.op("act", lambda e, hs=hs: e.activation(
            rows[:, 1, hs], PB[5][0:1, :], AF.Identity, scale=1.0 / D), inc="act")
        bank_free[4] = ("act", ta_last)
        bank_free[5] = ("act", ta_last)
    tick_bc = ln_rows_and_bcast(ta_last)
    x1_tick = None
    for m in range(NCH):
        for half in range(2):
            j = 2 * half + (m % 2)
            r1sl = T1[:, m * T + half * 512: m * T + half * 512 + 512]
            x1sl = T2[:, m * T + half * 512: m * T + half * 512 + 512]
            plan.w("dve", "dve", tick_bc)
            tq = plan.op("dve", lambda e, j=j, r1sl=r1sl, half=half: e.tensor_mul(
                scrC[:, j, :], r1sl, bc[:, 0, half * 512:(half + 1) * 512]), inc="dve")
            plan.w("dve", "dve", tq)
            x1_tick = plan.op("dve", lambda e, j=j, x1sl=x1sl, half=half: e.tensor_sub(
                x1sl, scrC[:, j, :], bc[:, 1, half * 512:(half + 1) * 512]), inc="dve")

    if STOP == "x1":
        dump_and_finish([T2[:, :]], ("dve", x1_tick))
        return finish(nc, plan)

    comb_tick = None
    gsc_free = {}
    n_tt = 1 if STOP == "comb0" else 8
    for tt in range(n_tt):
        par = tt % 2
        k = par
        acquire(k)
        plan.w("pe", "dve", x1_tick)
        plan.w("pe", "dmap", tick_x0)
        tg = None
        for dc in range(NCH):
            tg = plan.op("pe", lambda e, k=k, dc=dc, tt=tt: e.matmul(
                PB[k][:, 0:32], T2[:, dc * T + tt * 128: dc * T + tt * 128 + 128],
                gate_sb[:, dc, :], start=(dc == 0), stop=(dc == 15)),
                inc="pe" if dc == 15 else None)
        plan.w("act", "pe", tg)
        plan.wt("act", gsc_free.get(par))
        tsg = plan.op("act", lambda e, par=par, k=k: e.activation(
            gsc[:, par, :], PB[k][:, 0:32], AF.Sigmoid), inc="act")
        bank_free[k] = ("act", tsg)
        plan.w("dve", "act", tsg)
        tmx = plan.op("dve", lambda e, par=par: e.max(top8[:, :], gsc[:, par, :]), inc="dve")
        plan.w("dve", "dve", tmx)
        tq = plan.op("dve", lambda e, par=par: e.tensor_single_scalar(
            mskt[:, par, :], gsc[:, par, :], top8[:, 5:6], ALU.is_ge), inc="dve")
        plan.w("dve", "dve", tq)
        tq = plan.op("dve", lambda e, par=par: e.tensor_mul(
            kept[:, par, :], mskt[:, par, :], gsc[:, par, :]), inc="dve")
        plan.w("dve", "dve", tq)
        tq = plan.op("dve", lambda e, par=par: e.tensor_reduce(
            den[:, par:par + 1], kept[:, par, :], AX.X, ALU.add), inc="dve")
        plan.w("dve", "dve", tq)
        tq = plan.op("dve", lambda e, par=par: e.tensor_scalar_add(
            den[:, par:par + 1], den[:, par:par + 1], 1e-20), inc="dve")
        plan.w("dve", "dve", tq)
        trd = plan.op("dve", lambda e, par=par: e.reciprocal(
            den[:, par:par + 1], den[:, par:par + 1]), inc="dve")
        plan.w("dve", "dve", trd)
        tc1 = plan.op("dve", lambda e, par=par: e.tensor_single_scalar(
            ctm[:, par, :], kept[:, par, :], den[:, par:par + 1], ALU.mult), inc="dve")
        gsc_free[par] = ("dve", tc1)
        k2 = 2 + par
        acquire(k2)
        plan.w("pe", "dve", tc1)
        tt2 = plan.op("pe", lambda e, k2=k2, par=par: e.transpose(
            PB[k2][0:32, 0:128], ctm[:, par, :], id_f[:, :]), inc="pe")
        plan.w("dve", "pe", tt2)
        comb_tick = plan.op("dve", lambda e, k2=k2, tt=tt: e.tensor_copy(
            call[:, tt * 128:(tt + 1) * 128], PB[k2][0:32, 0:128]), inc="dve")
        bank_free[k2] = ("dve", comb_tick)

    if STOP == "comb0":
        dump_and_finish([gsc[:, 0, :], top8[:, :], mskt[:, 0, :], kept[:, 0, :],
                         den[:, 0:1], ctm[:, 0, :], call[:, 0:1024]],
                        ("dve", plan.cnt["dve"]))
        return finish(nc, plan)
    if STOP == "comb":
        dump_and_finish([call[:, :]], ("dve", comb_tick))
        return finish(nc, plan)

    up_rot = [0]
    dn_rot = [0]
    act_lastuse = [None] * 12
    act_ready = {}
    crow_last_read = [None]
    cb_ticks = [None, None]
    silu_free = {}
    acc_tick = {}

    groups = [[("sh", fc, fc) for fc in range(12)]]
    for e0 in range(0, NE, 2):
        groups.append([(e0, fc, fc) for fc in range(6)] +
                      [(e0 + 1, fc, 6 + fc) for fc in range(6)])

    def acc_ap(dc, half):
        base = acc_lo if dc < 8 else acc_hi
        return base[:, dc % 8, half * 512:(half + 1) * 512]

    for gi, entries in enumerate(groups):
        is_sh = entries[0][0] == "sh"
        if not is_sh:
            e0 = entries[0][0]
            for e_loc in range(2):
                e = e0 + e_loc
                plan.wt("sp", crow_last_read[0])
                plan.w("sp", "dve", comb_tick)
                trow = plan.op("sp", lambda e_, e=e: e_.dma_start(crow[:, :], call[e:e + 1, :]),
                               inc="dmas", incn=16)
                for half in range(2):
                    k = up_rot[0] % 4
                    up_rot[0] += 1
                    acquire(k)
                    plan.w("pe", "dmas", trow)
                    tb = plan.op("pe", lambda e_, k=k, half=half: e_.matmul(
                        PB[k][:, :], ones_f1[:, :], crow[:, half * 512:(half + 1) * 512],
                        start=True, stop=True), inc="pe")
                    crow_last_read[0] = ("pe", tb)
                    plan.w("dve", "pe", tb)
                    tcb = plan.op("dve", lambda e_, k=k, e_loc=e_loc, half=half: e_.tensor_copy(
                        bc[:, e_loc, half * 512:(half + 1) * 512], PB[k][:, :]), inc="dve")
                    bank_free[k] = ("dve", tcb)
                    cb_ticks[e_loc] = tcb

        for (esrc, fc, slot) in entries:
            if esrc == "sh":
                gsrc = bass.AP(sg, fc * 128, [[FS, 128], [128 * FS, 16], [1, 128]])
                usrc = bass.AP(su, fc * 128, [[FS, 128], [128 * FS, 16], [1, 128]])
                e_loc = 0
            else:
                gsrc = bass.AP(wg, esrc * D * FE + fc * 128, [[FE, 128], [128 * FE, 16], [1, 128]])
                usrc = bass.AP(wu, esrc * D * FE + fc * 128, [[FE, 128], [128 * FE, 16], [1, 128]])
                e_loc = esrc % 2
            sgu, twg = gu_dma(gsrc)
            suu, twu = gu_dma(usrc)
            for half in range(2):
                kg = up_rot[0] % 4
                up_rot[0] += 1
                ku = up_rot[0] % 4
                up_rot[0] += 1
                acquire(kg)
                plan.wt("pe", twg)
                plan.w("pe", "dve", x1_tick)
                tg = None
                for dc in range(NCH):
                    tg = plan.op("pe", lambda e, kg=kg, sgu=sgu, dc=dc, half=half: e.matmul(
                        PB[kg][:, :], RG[:, sgu, dc, :],
                        T2[:, dc * T + half * 512: dc * T + half * 512 + 512],
                        start=(dc == 0), stop=(dc == 15)), inc="pe" if dc == 15 else None)
                acquire(ku)
                plan.wt("pe", twu)
                tu = None
                for dc in range(NCH):
                    tu = plan.op("pe", lambda e, ku=ku, suu=suu, dc=dc, half=half: e.matmul(
                        PB[ku][:, :], RG[:, suu, dc, :],
                        T2[:, dc * T + half * 512: dc * T + half * 512 + 512],
                        start=(dc == 0), stop=(dc == 15)), inc="pe" if dc == 15 else None)
                if half == 1:
                    gu_free[sgu] = ("pe", tg)
                    gu_free[suu] = ("pe", tu)
                j = up_rot[0] % 4
                j2 = (j + 1) % 4
                plan.w("act", "pe", tg)
                plan.wt("act", silu_free.get(j))
                tsil = plan.op("act", lambda e, j=j, kg=kg: e.activation(
                    silu4[:, j, :], PB[kg][:, :], AF.Silu), inc="act")
                bank_free[kg] = ("act", tsil)
                aslot = actp[:, slot, half * 512:(half + 1) * 512]
                plan.w("dve", "act", tsil)
                plan.w("dve", "pe", tu)
                plan.wt("dve", act_lastuse[slot])
                if is_sh:
                    tac = plan.op("dve", lambda e, a=aslot, j=j, ku=ku: e.tensor_mul(
                        a, silu4[:, j, :], PB[ku][:, :]), inc="dve")
                else:
                    tq = plan.op("dve", lambda e, j=j, j2=j2, ku=ku: e.tensor_mul(
                        silu4[:, j2, :], silu4[:, j, :], PB[ku][:, :]), inc="dve")
                    plan.w("dve", "dve", tq)
                    plan.w("dve", "dve", cb_ticks[e_loc])
                    tac = plan.op("dve", lambda e, a=aslot, j2=j2, e_loc=e_loc, half=half: e.tensor_mul(
                        a, silu4[:, j2, :], bc[:, e_loc, half * 512:(half + 1) * 512]),
                        inc="dve")
                silu_free[j] = ("dve", tac)
                bank_free[ku] = ("dve", tac)
                act_ready[(slot, half)] = tac

        group_last_mm = None
        for dq in range(4):
            wd_slots = []
            for (esrc, fc, slot) in entries:
                if esrc == "sh":
                    src = bass.AP(sd, fc * 128 * D + dq * 512, [[D, 128], [1, 512]])
                else:
                    src = bass.AP(wd, esrc * FE * D + fc * 128 * D + dq * 512,
                                  [[D, 128], [1, 512]])
                wd_slots.append(wd_dma(src))
            dq_last_mm = None
            for half in range(2):
                for dcl in range(4):
                    dc = dq * 4 + dcl
                    k = 4 + dn_rot[0] % 2
                    dn_rot[0] += 1
                    acquire(k)
                    tdn = None
                    for i, (esrc, fc, slot) in enumerate(entries):
                        ws, tw = wd_slots[i]
                        plan.wt("pe", tw)
                        plan.w("pe", "dve", act_ready[(slot, half)])
                        tdn = plan.op("pe", lambda e, k=k, ws=ws, dcl=dcl, i=i, slot=slot,
                                      half=half: e.matmul(
                            PB[k][:, :], RW[:, ws, dcl * 128:(dcl + 1) * 128],
                            actp[:, slot, half * 512:(half + 1) * 512],
                            start=(i == 0), stop=(i == 11)),
                            inc="pe" if i == 11 else None)
                    dq_last_mm = tdn
                    plan.w("dve", "pe", tdn)
                    if gi == 0:
                        tev = plan.op("dve", lambda e, k=k, dc=dc, half=half: e.tensor_copy(
                            acc_ap(dc, half), PB[k][:, :]), inc="dve")
                    else:
                        tev = plan.op("dve", lambda e, k=k, dc=dc, half=half: e.tensor_tensor(
                            acc_ap(dc, half), PB[k][:, :], acc_ap(dc, half), ALU.add),
                            inc="dve")
                    bank_free[k] = ("dve", tev)
                    acc_tick[(dc, half)] = tev
            for ws, tw in wd_slots:
                wd_free[ws] = ("pe", dq_last_mm)
            group_last_mm = dq_last_mm
        for (esrc, fc, slot) in entries:
            act_lastuse[slot] = ("pe", group_last_mm)

    if STOP == "acc":
        tl = max(acc_tick.values())
        dump_and_finish([acc_lo.ap().rearrange("p a b -> p (a b)"),
                         acc_hi.ap().rearrange("p a b -> p (a b)")], ("dve", tl))
        return finish(nc, plan)

    r2_ticks = {}
    for dc in range(NCH):
        for half in range(2):
            a = acc_ap(dc, half)
            x1sl = T2[:, dc * T + half * 512: dc * T + half * 512 + 512]
            plan.w("dve", "dve", acc_tick[(dc, half)])
            r2_ticks[(dc, half)] = plan.op(
                "dve", lambda e, a=a, x1sl=x1sl: e.tensor_tensor(a, a, x1sl, ALU.add),
                inc="dve")
    sq2_free = {}
    ta_last = None
    for half in range(2):
        acquire(4)
        acquire(5)
        tmm2 = None
        for dc in range(NCH):
            a = acc_ap(dc, half)
            j = dc % 2
            plan.w("dve", "dve", r2_ticks[(dc, half)])
            plan.wt("dve", sq2_free.get(j))
            tr2 = plan.op("dve", lambda e, a=a, j=j: e.tensor_copy(sqr2[:, j, :], a),
                          inc="dve")
            plan.w("act", "dve", tr2)
            tsq = plan.op("act", lambda e, j=j: e.activation(
                sqr2[:, 2 + j, :], sqr2[:, j, :], AF.Square), inc="act")
            plan.w("pe", "dve", tr2)
            plan.op("pe", lambda e, j=j, dc=dc: e.matmul(
                PB[4][0:1, :], ones_bf[:, :], sqr2[:, j, :],
                start=(dc == 0), stop=(dc == 15)))
            plan.w("pe", "act", tsq)
            tmm2 = plan.op("pe", lambda e, j=j, dc=dc: e.matmul(
                PB[5][0:1, :], ones_bf[:, :], sqr2[:, 2 + j, :],
                start=(dc == 0), stop=(dc == 15)), inc="pe")
            sq2_free[j] = ("pe", tmm2)
        hs = slice(half * 512, (half + 1) * 512)
        plan.w("act", "pe", tmm2)
        plan.op("act", lambda e, hs=hs: e.activation(
            rows[:, 0, hs], PB[4][0:1, :], AF.Identity, scale=1.0 / D))
        ta_last = plan.op("act", lambda e, hs=hs: e.activation(
            rows[:, 1, hs], PB[5][0:1, :], AF.Identity, scale=1.0 / D), inc="act")
        bank_free[4] = ("act", ta_last)
        bank_free[5] = ("act", ta_last)
    tick_bc2 = ln_rows_and_bcast(ta_last)

    for dc in range(NCH):
        tdone = None
        for half in range(2):
            a = acc_ap(dc, half)
            hs = slice(half * 512, (half + 1) * 512)
            plan.w("dve", "dve", tick_bc2)
            tq = plan.op("dve", lambda e, a=a, hs=hs: e.tensor_tensor(
                a, a, bc[:, 0, hs], ALU.mult), inc="dve")
            plan.w("dve", "dve", tq)
            tdone = plan.op("dve", lambda e, a=a, hs=hs: e.tensor_tensor(
                a, a, bc[:, 1, hs], ALU.subtract), inc="dve")
        base = acc_lo if dc < 8 else acc_hi
        plan.w("sp", "dve", tdone)
        plan.op("sp", lambda e, dc=dc, base=base: e.dma_start(
            yt[dc * 128:(dc + 1) * 128, :], base[:, dc % 8, :]), inc="dmas", incn=16)

    plan.cmds["sp"].append(("wait", "dmas", plan.cnt["dmas"]))
    return finish(nc, plan)


def finish(nc, plan):
    from contextlib import ExitStack
    sem_names = sorted(set(plan.cnt.keys())
                       | {s for (_, s) in plan.lastw.keys()})
    with ExitStack() as stack:
        sems = {n: stack.enter_context(nc.semaphore(f"sem_{n}")) for n in sem_names}
        block = stack.enter_context(nc.Block())

        def replay(eng_obj, key):
            for c in plan.cmds[key]:
                if c[0] == "wait":
                    eng_obj.wait_ge(sems[c[1]], c[2])
                else:
                    _, fn, inc, incn = c
                    inst = fn(eng_obj)
                    if inc is not None:
                        inst.then_inc(sems[inc], incn)

        @block.tensor
        def _(t):
            replay(t, "pe")

        @block.vector
        def _(v):
            replay(v, "dve")

        @block.scalar
        def _(s):
            replay(s, "act")

        @block.gpsimd
        def _(g):
            replay(g, "pool")

        @block.sync
        def _(s):
            replay(s, "sp")

    return nc


_NC_CACHE = [None]


def kernel(**inp):
    a = {k: np.asarray(v) for k, v in inp.items()}
    x = np.ascontiguousarray(a["L0_attn_x"], dtype=np.float32).reshape(8192, D)
    br = {k: np.ascontiguousarray(a[f"L0_attn_b{k}"].reshape(16, 128).T) for k in "vo"}
    shared = dict(
        wvt=np.ascontiguousarray(a["L0_attn_Wv"].T),
        wot=np.ascontiguousarray(a["L0_attn_Wo"].T),
        bvr=br["v"], bor=br["o"],
        gw=np.ascontiguousarray(a["L0_ffn_gate_w"]),
        wg=np.ascontiguousarray(a["L0_ffn_wg"]),
        wu=np.ascontiguousarray(a["L0_ffn_wu"]),
        wd=np.ascontiguousarray(a["L0_ffn_wd"]),
        sg=np.ascontiguousarray(a["L0_ffn_sg"]),
        su=np.ascontiguousarray(a["L0_ffn_su"]),
        sd=np.ascontiguousarray(a["L0_ffn_sd"]),
    )
    in_maps = []
    for c in range(8):
        m = dict(shared)
        m["xt"] = np.ascontiguousarray(x[c * T:(c + 1) * T, :].T)
        in_maps.append(m)

    if _NC_CACHE[0] is None:
        _NC_CACHE[0] = build_nc()
    res = run_bass_kernel_spmd(_NC_CACHE[0], in_maps, core_ids=list(range(8)))
    out = np.empty((8192, D), dtype=np.float32)
    for c in range(8):
        out[c * T:(c + 1) * T, :] = res.results[c]["yt"].T
    return out.reshape(4, 2048, D)
